# version 03
# baseline (speedup 1.0000x reference)
import contextlib
from dataclasses import dataclass, field

ENGINES = ("tensor", "vector", "scalar", "gpsimd", "sync")


@dataclass
class _Op:
    idx: int
    engine: str
    fn: object
    reads: tuple
    writes: tuple
    is_dma: bool
    tag: str
    needs_inc: bool = False
    inc_idx: int = 0
    sem_slot: int = -1
    sem_count: int = 0
    waits: list = field(default_factory=list)


class Prog:
    def __init__(self, nc, n_dma_sems=24):
        self.nc = nc
        self.ops = []
        self.n_dma_sems = n_dma_sems

    def op(self, engine, fn, reads=(), writes=(), tag=""):
        assert engine in ENGINES, engine
        o = _Op(len(self.ops), engine, fn, tuple(reads), tuple(writes), False, tag)
        self.ops.append(o)
        return o

    def dma(self, engine, fn, reads=(), writes=(), tag=""):
        assert engine in ENGINES, engine
        o = _Op(len(self.ops), engine, fn, tuple(reads), tuple(writes), True, tag)
        self.ops.append(o)
        return o

    def copy_dma(self, engine, out_ap, in_ap, tag="", **kw):
        def fn(eng, out_ap=out_ap, in_ap=in_ap, kw=kw):
            return eng.dma_start(out=out_ap, in_=in_ap, **kw)
        return fn

    @staticmethod
    def _exclusive(key):
        return isinstance(key[0], str) and key[0].startswith("ps:")

    def _resolve(self):
        last_writer = {}
        readers_since = {}
        for o in self.ops:
            deps = set()
            for r in o.reads:
                w = last_writer.get(r)
                if w is not None:
                    deps.add(w)
                if self._exclusive(r):
                    for rd in readers_since.get(r, ()):
                        deps.add(rd)
            for w_key in o.writes:
                w = last_writer.get(w_key)
                if w is not None:
                    deps.add(w)
                for rd in readers_since.get(w_key, ()):
                    deps.add(rd)
            o._deps = set()
            for d in deps:
                dop = self.ops[d]
                if dop is o:
                    continue
                if dop.is_dma:
                    o._deps.add(d)
                elif dop.engine != o.engine:
                    o._deps.add(d)
                elif o.engine in ("vector", "scalar", "gpsimd") and not o.is_dma:
                    o._deps.add(d)
            for w_key in o.writes:
                last_writer[w_key] = o.idx
                readers_since[w_key] = []
            for r in o.reads:
                readers_since.setdefault(r, []).append(o.idx)

        for o in self.ops:
            for d in o._deps:
                self.ops[d].needs_inc = True

        inc_count = {e: 0 for e in ENGINES}
        slot_map = {}
        swdge_n = [0]
        dma_use = {}
        for o in self.ops:
            if o.is_dma:
                o.needs_inc = True
                o.swdge_clear = False
                if o.engine == "gpsimd":
                    bk = ("_sw", swdge_n[0])
                    swdge_n[0] += 1
                else:
                    if o.writes:
                        bk = o.writes[0]
                    elif o.reads:
                        bk = ("_st",) + tuple(o.reads[0])
                    else:
                        bk = ("_stores",)
                if bk not in slot_map:
                    slot_map[bk] = len(slot_map)
                o.sem_slot = slot_map[bk]
                o.sem_epoch = 0
                dma_use[o.sem_slot] = dma_use.get(o.sem_slot, 0) + 1
                o.sem_count = dma_use[o.sem_slot]
            elif o.needs_inc:
                inc_count[o.engine] += 1
                o.inc_idx = inc_count[o.engine]

        waited = {}
        for o in self.ops:
            ws = {}
            for d in sorted(o._deps):
                dop = self.ops[d]
                if dop.is_dma:
                    key = ("dma", dop.sem_slot, getattr(dop, "sem_epoch", 0))
                    val = 16 * dop.sem_count
                else:
                    key = ("eng", dop.engine)
                    val = dop.inc_idx
                    assert val > 0
                if ws.get(key, 0) < val:
                    ws[key] = val
            for key, val in ws.items():
                prev = waited.get((o.engine, key), 0)
                if val > prev:
                    o.waits.append((key, val))
                    waited[(o.engine, key)] = val
        self._inc_totals = inc_count

    def finalize(self, final_waits_engine="sync"):
        self._resolve()
        nc = self.nc
        self._stack = contextlib.ExitStack()
        eng_sems = {}
        for e in ENGINES:
            if self._inc_totals[e] > 0:
                eng_sems[e] = self._stack.enter_context(nc.semaphore(f"sem_{e}"))
        used_slots = sorted({o.sem_slot for o in self.ops if o.is_dma})
        dma_sems = {s: self._stack.enter_context(nc.semaphore(f"dmas_{s}"))
                    for s in used_slots}
        dma_totals = {}
        for o in self.ops:
            if o.is_dma:
                dma_totals[o.sem_slot] = (getattr(o, "sem_epoch", 0), o.sem_count)

        by_engine = {e: [o for o in self.ops if o.engine == e] for e in ENGINES}
        engines_used = [e for e in ENGINES if by_engine[e]]
        if final_waits_engine not in engines_used:
            engines_used.append(final_waits_engine)

        def sem_of(key):
            if key[0] == "dma":
                return dma_sems[key[1]]
            return eng_sems[key[1]]

        block = self._stack.enter_context(nc.Block())
        for e in engines_used:
            def body(eng, e=e):
                for o in by_engine[e]:
                    for key, val in o.waits:
                        eng.wait_ge(sem_of(key), val)
                    inst = o.fn(eng)
                    if o.needs_inc:
                        assert inst is not None, f"op {o.idx} {o.tag}: fn returned None but needs inc"
                        if o.is_dma:
                            inst.then_inc(dma_sems[o.sem_slot], 16)
                        elif e in ("vector", "scalar", "gpsimd"):
                            eng.drain().then_inc(eng_sems[e], 1)
                        else:
                            inst.then_inc(eng_sems[e], 1)
                if e == final_waits_engine:
                    waited = {}
                    for o in by_engine[e]:
                        for key, val in o.waits:
                            waited[key] = max(waited.get(key, 0), val)
                    for s, (ep, tot) in sorted(dma_totals.items()):
                        if waited.get(("dma", s, ep), 0) < 16 * tot:
                            eng.wait_ge(dma_sems[s], 16 * tot)
            getattr(block, e)(body)
        self._stack.close()
        return nc

    def stats(self):
        from collections import Counter
        c = Counter(o.engine for o in self.ops)
        d = Counter(o.engine for o in self.ops if o.is_dma)
        w = sum(len(o.waits) for o in self.ops)
        i = sum(1 for o in self.ops if o.needs_inc)
        return f"ops={len(self.ops)} per-engine={dict(c)} dma={dict(d)} waits={w} incs={i}"


"""mk.py — phase builders for the mk_macaron_d2 megakernel.

Layouts (per core, batch element):
  x, y streams: HBM [T=4096, 1024] fp32, processed in groups of G=512 tokens
  token-major tiles: [128 tokens, feat]; feature-major: [feat-chunk 128, tokens]
Conventions:
  dep keys: ("ps:<name>",) for PSUM tensors (exclusive access), else (name,)
  scratch tensors: allocated once by name, reused; dep key == name => serialized
  DMA queues: sync = plain loads/stores, gpsimd = fp32->bf16 cast loads
"""
import numpy as np
import ml_dtypes
import concourse.bass as bass
import concourse.mybir as mybir

F32 = mybir.dt.float32
BF16 = mybir.dt.bfloat16
AF = mybir.ActivationFunctionType
ALU = mybir.AluOpType
AX = mybir.AxisListType
BF = ml_dtypes.bfloat16

D = 1024
C23H = float(1.5 * 2 ** 23)
C22H = float(1.5 * 2 ** 22)
NEG = -1.0e38


def host_consts(T):
    c = {}
    c["ident"] = np.eye(128, dtype=BF)
    c["ones_row"] = np.ones((1, 512), dtype=BF)
    c["ones_f32"] = np.ones((1, 128), dtype=np.float32)
    half = np.arange(0, 64, 2, dtype=np.float32)
    inv_freq = (1.0 / (10000.0 ** (half / 64.0))).astype(np.float32)
    t = np.arange(T, dtype=np.float32)
    freqs = t[:, None] * inv_freq[None, :]
    freqs = np.concatenate([freqs, freqs], -1)
    base_scale = ((half + 0.4 * 64.0) / (1.4 * 64.0)).astype(np.float32)
    power = (t - T // 2) / (256 // 2)
    scale = base_scale[None, :] ** power[:, None]
    scale = np.concatenate([scale, scale], -1).astype(np.float32)
    cos = np.cos(freqs).astype(np.float32)
    sin = np.sin(freqs).astype(np.float32)
    sgn = np.where(np.arange(64) < 32, -1.0, 1.0).astype(np.float32)
    c["rotAq"] = (cos * scale).astype(BF)
    c["rotBq"] = (sgn[None, :] * sin * scale).astype(BF)
    inv = (1.0 / scale).astype(np.float32)
    c["rotAk"] = (cos * inv).astype(BF)
    c["rotBk"] = (sgn[None, :] * sin * inv).astype(BF)

    def mk_mask(width, valid):
        m = np.zeros((128, width), np.float32)
        for i in range(128):
            lo, hi = valid(i)
            m[i, :max(0, lo)] = NEG
            m[i, min(width, hi + 1):] = NEG
        return m.astype(BF)
    c["m128c"] = mk_mask(128, lambda i: (0, i))
    c["m256"] = mk_mask(256, lambda i: (0, i + 128))
    c["m384f"] = mk_mask(384, lambda i: (i, i + 256))
    c["m384a"] = mk_mask(384, lambda i: (i + 1, i + 256))
    return c


class B:
    def __init__(self, name, T=4096):
        self.nc = bass.Bass("TRN2", target_bir_lowering=False)
        self.p = Prog(self.nc, n_dma_sems=24)
        self.name = name
        self.T = T
        self.NG = T // 512
        self.consts = host_consts(T)
        self.inputs = {}
        self._scr = {}
        self._cst = {}

    def inp(self, name, shape, dtype=F32, const=None):
        h = self.nc.dram_tensor(name, list(shape), dtype, kind="ExternalInput").ap()
        self.inputs[name] = const
        return h

    def outp(self, name, shape, dtype=F32):
        return self.nc.dram_tensor(name, list(shape), dtype, kind="ExternalOutput").ap()

    def const(self, name):
        if name in self._cst:
            return self._cst[name]
        arr = self.consts[name]
        dt = BF16 if arr.dtype == BF else F32
        h = self.inp("cst_" + name, arr.shape, dt, const=arr)
        t = self.sb("c_" + name, arr.shape, dt)
        key = ("c_" + name,)
        self.dma("sync", lambda e: e.dma_start(out=t[...], in_=h[...]), w=[key], tag="ld_" + name)
        self._cst[name] = (t, key)
        return t, key

    def sb(self, name, shape, dtype=F32):
        return self.nc.alloc_sbuf_tensor(name, list(shape), dtype)

    def ps(self, name, shape, dtype=F32):
        return self.nc.alloc_psum_tensor(name, list(shape), dtype)

    def scratch(self, name, shape, dtype=F32, psum=False):
        if name not in self._scr:
            t = self.ps(name, shape, dtype) if psum else self.sb(name, shape, dtype)
            self._scr[name] = (t, (("ps:" + name) if psum else name,))
        return self._scr[name]

    def op(self, eng, fn, r=(), w=(), tag=""):
        return self.p.op(eng, fn, reads=r, writes=w, tag=tag)

    def dma(self, eng, fn, r=(), w=(), tag=""):
        return self.p.dma(eng, fn, reads=r, writes=w, tag=tag)

    def ln_tile(self, pfx, halves, out_ap, key_out, eps=1e-5):
        bn6, k6 = self.scratch(pfx + "_bn6", (128, 12))
        bn2, k2 = self.scratch(pfx + "_bn2", (128, 2))
        rstd, kr = self.scratch(pfx + "_rstd", (128, 1))
        (apA, kA), (apB, kB) = halves
        self.op("vector", lambda e: e.bn_stats(bn6[:, 0:6], apA), r=[kA], w=[k6])
        self.op("vector", lambda e: e.bn_stats(bn6[:, 6:12], apB), r=[kB], w=[k6])
        self.op("vector", lambda e: e.bn_aggr(bn2[:, :], bn6[:, :]), r=[k6], w=[k2])
        self.op("vector", lambda e: e.tensor_scalar(out=rstd[:, :], in0=bn2[:, 1:2],
                    scalar1=float(eps), scalar2=None, op0=ALU.add), r=[k2], w=[kr])
        self.op("scalar", lambda e: e.activation(out=rstd[:, :], in_=rstd[:, :], func=AF.Ln),
                r=[kr], w=[kr])
        self.op("scalar", lambda e: e.activation(out=rstd[:, :], in_=rstd[:, :], func=AF.Exp, scale=-0.5),
                r=[kr], w=[kr])
        for half, (apH, kH) in enumerate(halves):
            o = out_ap[:, half * 512:(half + 1) * 512]
            self.op("vector", lambda e, apH=apH, o=o: e.tensor_scalar(
                        out=o, in0=apH, scalar1=bn2[:, 0:1], scalar2=rstd[:, :],
                        op0=ALU.subtract, op1=ALU.mult),
                    r=[kH, k2, kr], w=[key_out])

    def transposes(self, pfx, src_fn, nchunks, keys_r, out_ap_fn, key_out, evac="scalar",
                   dtype=BF16):
        assert nchunks <= 8
        ident, kid = self.const("ident")
        tp, kps = self.scratch(pfx + "_trp", (128, 128 * 8), dtype, psum=True)
        for c in range(nchunks):
            self.op("tensor", lambda e, c=c: e.transpose(
                        tp[:, c * 128:(c + 1) * 128], src_fn(c), ident[:, :]),
                    r=list(keys_r) + [kid], w=[kps], tag=pfx + f"_tr{c}")
        wdt = 128 * nchunks
        if evac == "scalar":
            self.op("scalar", lambda e: e.copy(out=out_ap_fn(), in_=tp[:, 0:wdt]), r=[kps], w=[key_out])
        else:
            self.op("vector", lambda e: e.tensor_copy(out=out_ap_fn(), in_=tp[:, 0:wdt]), r=[kps], w=[key_out])

    def quant_tile(self, pfx, x_ap, out_ap, key_in, key_out, C, Cmax=2816):
        nb = C // 16
        bm0, kbm = self.scratch(pfx + "_bm", (128, Cmax // 16))
        rs0, krs = self.scratch(pfx + "_rs", (128, Cmax // 16))
        xn0, kxn = self.scratch(pfx + "_xn", (128, Cmax))
        t10, kt1 = self.scratch(pfx + "_t1", (128, Cmax))
        t20, kt2 = self.scratch(pfx + "_t2", (128, Cmax))
        sg0, ksg = self.scratch(pfx + "_sg", (128, Cmax), BF16)
        bm, rs, xn, t1, t2, sg = (bm0[:, 0:nb], rs0[:, 0:nb], xn0[:, 0:C],
                                  t10[:, 0:C], t20[:, 0:C], sg0[:, 0:C])
        V, S = "vector", "scalar"
        self.op(V, lambda e: e.tensor_reduce(out=bm, in_=x_ap.rearrange("p (b s) -> p b s", s=16),
                    axis=AX.X, op=ALU.max, apply_absolute_value=True), r=[key_in], w=[kbm])
        self.op(V, lambda e: e.tensor_scalar(out=bm, in0=bm, scalar1=1.0 / 6.0,
                    scalar2=1e-12, op0=ALU.mult, op1=ALU.max), r=[kbm], w=[kbm])
        self.op(V, lambda e: e.reciprocal(out=rs, in_=bm), r=[kbm], w=[krs])
        self.op(V, lambda e: e.tensor_mul(xn.rearrange("p (b s) -> p b s", s=16),
                    x_ap.rearrange("p (b s) -> p b s", s=16),
                    rs.broadcast_to((128, nb, 16))), r=[key_in, krs], w=[kxn])
        self.op(S, lambda e: e.activation(out=t1, in_=xn, func=AF.Abs), r=[kxn], w=[kt1])
        self.op(S, lambda e: e.activation(out=sg, in_=xn, func=AF.Sign), r=[kxn], w=[ksg])
        self.op(V, lambda e: e.tensor_scalar(out=t2, in0=t1, scalar1=2.0, scalar2=C22H,
                    op0=ALU.min, op1=ALU.add), r=[kt1], w=[kt2])
        self.op(V, lambda e: e.tensor_scalar(out=t2, in0=t2, scalar1=C22H, scalar2=None,
                    op0=ALU.subtract), r=[kt2], w=[kt2])
        self.op(V, lambda e: e.tensor_scalar(out=xn, in0=t1, scalar1=4.0, scalar2=C23H,
                    op0=ALU.min, op1=ALU.add), r=[kt1, ksg], w=[kxn])
        self.op(V, lambda e: e.tensor_scalar(out=xn, in0=xn, scalar1=C23H + 2.0, scalar2=None,
                    op0=ALU.subtract), r=[kxn], w=[kxn])
        self.op(V, lambda e: e.scalar_tensor_tensor(out=xn, in0=xn, scalar=0.0,
                    in1=t2, op0=ALU.max, op1=ALU.add), r=[kt2, kxn], w=[kxn])
        self.op(V, lambda e: e.tensor_scalar(out=t2, in0=t1, scalar1=5.0, scalar2=None,
                    op0=ALU.is_gt), r=[kt1], w=[kt2])
        self.op(V, lambda e: e.scalar_tensor_tensor(out=xn, in0=t2, scalar=2.0,
                    in1=xn, op0=ALU.mult, op1=ALU.add), r=[kt2, kxn], w=[kxn])
        self.op(V, lambda e: e.tensor_mul(t1, xn, sg), r=[kxn, ksg], w=[kt1])
        self.op(V, lambda e: e.tensor_mul(out_ap.rearrange("p (b s) -> p b s", s=16),
                    t1.rearrange("p (b s) -> p b s", s=16),
                    bm.broadcast_to((128, nb, 16))), r=[kt1, kbm], w=[key_out])

    def load_group(self, hbm_ap, g, sbuf_t, key, cols=(0, 1024), row_off=0, dst_cols=None, tag="ldg", G=512):
        c0, c1 = cols
        r0 = g * G + row_off
        src = hbm_ap[r0:r0 + G, c0:c1].rearrange("(tt p) c -> p tt c", p=128)
        d0, d1 = dst_cols if dst_cols is not None else (0, c1 - c0)
        self.dma("sync", lambda e: e.dma_start(out=sbuf_t[:, :, d0:d1], in_=src), w=[key], tag=tag)

    def store_group(self, hbm_ap, g, sbuf_t, key, tag="stg", G=512):
        dst = hbm_ap[g * G:(g + 1) * G, :].rearrange("(tt p) c -> p tt c", p=128)
        self.dma("sync", lambda e: e.dma_start(out=dst, in_=sbuf_t[:, :, :]), r=[key], tag=tag)

    def cast_load(self, hbm_ap_slice, sbuf_ap, key, tag="cl"):
        self.dma("gpsimd", lambda e: e.dma_start(out=sbuf_ap, in_=hbm_ap_slice), w=[key], tag=tag)

    def bias_row_mm(self, psum_ap, brow_ap, keys_r, key_w, token_major, ones, kon, start=False, stop=True):
        n = psum_ap.shape[-1]
        if token_major:
            self.op("tensor", lambda e: e.matmul(psum_ap, ones[:, 0:128], brow_ap,
                        start=start, stop=stop), r=list(keys_r) + [kon], w=[key_w])
        else:
            self.op("tensor", lambda e: e.matmul(psum_ap, brow_ap, ones[:, 0:n],
                        start=start, stop=stop), r=list(keys_r) + [kon], w=[key_w])

    def finalize(self):
        self.p.finalize()
        return self.nc


def build_l0ffn(T=4096):
    b = B("l0ffn", T)
    NGl = T // 512
    x_h = b.inp("x", (T, D))
    wg_h = b.inp("w_gate", (2816, D))
    wu_h = b.inp("w_up", (2816, D))
    wd_h = b.inp("w_down", (D, 2816))
    xo_h = b.outp("xo", (T, D))

    wgT = b.sb("wgT", (128, 8, 2816), BF16); kwgT = ("wgT",)
    wuT = b.sb("wuT", (128, 8, 2816), BF16); kwuT = ("wuT",)
    wdT = b.sb("wdT", (128, 22, 1024), BF16); kwdT = ("wdT",)

    qin, kqin = b.scratch("qz_in", (128, 2816))
    wq, kwq = b.scratch("wq", (128, 2816), BF16)

    def quant(x_ap, out_ap, kin, kout, C):
        for h0 in range(0, C, 1408):
            w = min(1408, C - h0)
            b.quant_tile("qz", x_ap[:, h0:h0 + w], out_ap[:, h0:h0 + w], kin, kout, w, Cmax=1408)

    for (w_h, wT, kwT, nrc, Cw) in ((wg_h, wgT, kwgT, 22, D), (wu_h, wuT, kwuT, 22, D),
                                    (wd_h, wdT, kwdT, 8, 2816)):
        nblk = Cw // 128
        for rc in range(nrc):
            b.dma("sync", lambda e, w_h=w_h, rc=rc, Cw=Cw: e.dma_start(
                      out=qin[:, 0:Cw], in_=w_h[rc * 128:(rc + 1) * 128, :]),
                  w=[kqin], tag=f"wld{rc}")
            quant(qin, wq, kqin, kwq, Cw)
            for c0 in range(0, nblk, 8):
                cn = min(8, nblk - c0)
                b.transposes("trA", lambda c, c0=c0: wq[:, (c0 + c) * 128:(c0 + c + 1) * 128],
                             cn, [kwq],
                             lambda wT=wT, c0=c0, cn=cn, rc=rc: wT[:, c0:c0 + cn, rc * 128:(rc + 1) * 128],
                             kwT)

    xg, kxg = b.scratch("xg", (128, 4, D))
    xq, kxq = b.scratch("xq", (128, D), BF16)
    xqT, kxqT = b.scratch("xqT", (128, 8, 128), BF16)
    sgt, ksgt = b.scratch("sgt", (128, 512))
    hq, khq = b.scratch("hq", (128, 2816), BF16)
    hqT, khqT = b.scratch("hqT", (128, 22 * 128), BF16)
    g_ps, kg = b.scratch("g_ps", (128, 512), psum=True)
    u_ps, ku = b.scratch("u_ps", (128, 512), psum=True)
    d_ps, kd = b.scratch("d_ps", (128, D), psum=True)

    coltiles = [(i * 512, min(2816, (i + 1) * 512)) for i in range(6)]
    for g in range(NGl):
        b.load_group(x_h, g, xg, kxg, tag=f"ldx{g}")
        for tt in range(4):
            xt = xg[:, tt, :]
            b.ln_tile("ln", ((xt[:, 0:512], kxg), (xt[:, 512:1024], kxg)), qin[:, 0:D], kqin)
            quant(qin, xq, kqin, kxq, D)
            b.transposes("trA", lambda c: xq[:, c * 128:(c + 1) * 128], 8, [kxq],
                         lambda: xqT[:, :, :], kxqT)
            for (n0, n1) in coltiles:
                w = n1 - n0
                for ps, kps, wT, kwT in ((g_ps, kg, wgT, kwgT), (u_ps, ku, wuT, kwuT)):
                    for k in range(8):
                        b.op("tensor", lambda e, ps=ps, w=w, k=k, wT=wT, n0=n0, n1=n1:
                             e.matmul(ps[:, 0:w], xqT[:, k, :], wT[:, k, n0:n1],
                                      start=(k == 0), stop=(k == 7)),
                             r=[kxqT, kwT], w=[kps], tag=f"mm{n0}")
                b.op("scalar", lambda e, w=w: e.activation(out=sgt[:, 0:w], in_=g_ps[:, 0:w], func=AF.Silu),
                     r=[kg], w=[ksgt])
                b.op("vector", lambda e, n0=n0, n1=n1, w=w: e.tensor_mul(
                         qin[:, n0:n1], u_ps[:, 0:w], sgt[:, 0:w]), r=[ku, ksgt], w=[kqin])
            quant(qin, hq, kqin, khq, 2816)
            for c0 in range(0, 22, 8):
                cn = min(8, 22 - c0)
                b.transposes("trA", lambda c, c0=c0: hq[:, (c0 + c) * 128:(c0 + c + 1) * 128],
                             cn, [khq], lambda c0=c0, cn=cn: hqT[:, c0 * 128:(c0 + cn) * 128], khqT)
            for half in range(2):
                n0 = half * 512
                for k in range(22):
                    b.op("tensor", lambda e, k=k, n0=n0: e.matmul(
                             d_ps[:, n0:n0 + 512], hqT[:, k * 128:(k + 1) * 128],
                             wdT[:, k, n0:n0 + 512], start=(k == 0), stop=(k == 21)),
                         r=[khqT, kwdT], w=[kd], tag=f"dn{half}")
            b.op("vector", lambda e, tt=tt: e.scalar_tensor_tensor(
                     out=xg[:, tt, :], in0=d_ps[:, :], scalar=0.5, in1=xg[:, tt, :],
                     op0=ALU.mult, op1=ALU.add), r=[kd, kxg], w=[kxg])
        b.store_group(xo_h, g, xg, kxg, tag=f"stx{g}")
    return b


def build_l0attn(T=4096):
    b = B("l0attn", T)
    NGl = T // 512
    x_h = b.inp("x", (T, D))
    w_hs = {n: b.inp(n, (D, D)) for n in ("Wq", "Wk", "Wv", "Wo")}
    nc_h = b.inp("norm_const", (1, 16, 1, 1))
    xo_h = b.outp("xo", (T, D))

    wTs = {}
    wldb, kwldb = b.scratch("wldb", (128, D), BF16)
    for n in w_hs:
        wT = b.sb(n + "T", (128, 8, D), BF16)
        kwT = (n + "T",)
        wTs[n] = (wT, kwT)
        for rc in range(8):
            b.cast_load(w_hs[n][rc * 128:(rc + 1) * 128, :], wldb[:, :], kwldb, tag=f"c{n}{rc}")
            b.transposes("trA", lambda c: wldb[:, c * 128:(c + 1) * 128], 8, [kwldb],
                         lambda wT=wT, rc=rc: wT[:, :, rc * 128:(rc + 1) * 128], kwT)

    onesf, konf = b.const("ones_f32")
    frow, kfr = b.scratch("frow", (1, 16))
    fbc_ps, kfb_ps = b.scratch("fb_ps", (128, 16), psum=True)
    fbc, kfb = b.scratch("fbc", (128, 16))
    b.dma("sync", lambda e: e.dma_start(out=frow[:, :], in_=nc_h.rearrange("a h c d -> a (h c d)")),
          w=[kfr], tag="ldnc")
    b.op("scalar", lambda e: e.activation(out=frow[:, :], in_=frow[:, :], func=AF.Sigmoid), r=[kfr], w=[kfr])
    b.op("scalar", lambda e: e.activation(out=frow[:, :], in_=frow[:, :], func=AF.Exp,
            scale=-float(np.log(T))), r=[kfr], w=[kfr])
    b.op("tensor", lambda e: e.matmul(fbc_ps[:, :], onesf[:, :], frow[:, :], start=True, stop=True),
         r=[konf, kfr], w=[kfb_ps])
    b.op("scalar", lambda e: e.copy(out=fbc[:, :], in_=fbc_ps[:, :]), r=[kfb_ps], w=[kfb])

    xg, kxg = b.scratch("xg", (128, 4, D))
    xnb, kxnb = b.scratch("xnb", (128, D), BF16)
    xnT, kxnT = b.scratch("xnT", (128, 8, 128), BF16)
    qf, kqf = b.scratch("qf", (128, D))
    kf, kkf = b.scratch("kf", (128, D))
    sqv, ksqv = b.scratch("sqv", (128, D), BF16)
    ssq, kssq = b.scratch("ssq", (128, 16))
    rstd, krst = b.scratch("rstdh", (128, 16))
    qhb, kqhb = b.scratch("qhb", (128, D), BF16)
    khb, kkhb = b.scratch("khb", (128, D), BF16)
    vb, kvb = b.scratch("vb", (128, D), BF16)
    qhT = b.sb("qhT", (128, 8, T), BF16); kqhT = ("qhT",)
    p_ps, kpps = b.scratch("p_ps", (128, 512), psum=True)
    kv_ps, kkv = b.scratch("kv_ps", (128, 1024), psum=True)
    kvb_sb, kkvb = b.scratch("kvb_sb", (128, 8, 128), BF16)
    at_ps, kat = b.scratch("at_ps", (128, 512), psum=True)
    atb, katb = b.scratch("atb", (128, 8, 512), BF16)
    o_ps, kops = b.scratch("o_ps", (128, D), psum=True)

    def l2n(src_f, ksrc, dst_b, kdst, clamp=True):
        b.op("vector", lambda e: e.tensor_mul(sqv[:, :], src_f[:, :], src_f[:, :]), r=[ksrc], w=[ksqv])
        b.op("vector", lambda e: e.tensor_reduce(out=ssq[:, :],
                in_=sqv[:, :].rearrange("p (h d) -> p h d", d=64), axis=AX.X, op=ALU.add),
             r=[ksqv], w=[kssq])
        b.op("vector", lambda e: e.tensor_scalar(out=ssq[:, :], in0=ssq[:, :], scalar1=1e-24,
                scalar2=None, op0=ALU.max), r=[kssq], w=[kssq])
        b.op("scalar", lambda e: e.activation(out=rstd[:, :], in_=ssq[:, :], func=AF.Ln), r=[kssq], w=[krst])
        b.op("scalar", lambda e: e.activation(out=rstd[:, :], in_=rstd[:, :], func=AF.Exp, scale=-0.5),
             r=[krst], w=[krst])
        b.op("vector", lambda e: e.tensor_mul(dst_b[:, :].rearrange("p (h d) -> p h d", d=64),
                src_f[:, :].rearrange("p (h d) -> p h d", d=64),
                rstd[:, :].broadcast_to((128, 16, 64))), r=[ksrc, krst], w=[kdst])

    for g in range(NGl):
        b.load_group(x_h, g, xg, kxg, tag=f"ldxA{g}")
        for tt in range(4):
            xt = xg[:, tt, :]
            b.ln_tile("ln", ((xt[:, 0:512], kxg), (xt[:, 512:1024], kxg)), xnb[:, :], kxnb)
            b.transposes("trA", lambda c: xnb[:, c * 128:(c + 1) * 128], 8, [kxnb],
                         lambda: xnT[:, :, :], kxnT)
            for (dst, kdst, wn, fp) in ((qf, kqf, "Wq", True), (kf, kkf, "Wk", True), (vb, kvb, "Wv", False)):
                wT, kwT = wTs[wn]
                for m in range(2):
                    for k in range(8):
                        b.op("tensor", lambda e, m=m, k=k, wT=wT: e.matmul(
                                 p_ps[:, :], xnT[:, k, :], wT[:, k, m * 512:(m + 1) * 512],
                                 start=(k == 0), stop=(k == 7)), r=[kxnT, kwT], w=[kpps])
                    b.op("scalar", lambda e, dst=dst, m=m: e.copy(out=dst[:, m * 512:(m + 1) * 512],
                             in_=p_ps[:, :]), r=[kpps], w=[kdst])
            l2n(qf, kqf, qhb, kqhb)
            l2n(kf, kkf, khb, kkhb)
            for j in range(16):
                first = (g == 0 and tt == 0 and j in (0, 8))
                pr = j // 2
                stop = (g == NGl - 1 and tt == 3 and j in (7, 15))
                b.op("tensor", lambda e, j=j, pr=pr, first=first, stop=stop: e.matmul(
                         kv_ps[:, j * 64:(j + 1) * 64],
                         khb[:, pr * 128:(pr + 1) * 128], vb[:, j * 64:(j + 1) * 64],
                         start=first, stop=stop,
                         skip_group_check=True),
                     r=[kkhb, kvb], w=[kkv], tag=f"kv{j}")
            b.transposes("trA", lambda c: qhb[:, c * 128:(c + 1) * 128], 8, [kqhb],
                         lambda g=g, tt=tt: qhT[:, :, g * 512 + tt * 128:g * 512 + (tt + 1) * 128], kqhT)

    import os
    if os.environ.get("MKDEBUG"):
        qhT_h = b.outp("dbg_qhT", (128, 8, T), BF16)
        b.dma("sync", lambda e: e.dma_start(out=qhT_h[:, :, :], in_=qhT[:, :, :]), r=[kqhT], tag="dbgq")
        kv_h = b.outp("dbg_kv", (128, 1024))
        kvsb2, kkvsb2 = b.scratch("kvsb2", (128, 1024))
        b.op("scalar", lambda e: e.copy(out=kvsb2[:, :], in_=kv_ps[:, :]), r=[kkv], w=[kkvsb2])
        b.dma("sync", lambda e: e.dma_start(out=kv_h[:, :], in_=kvsb2[:, :]), r=[kkvsb2], tag="dbgkv")
    b.op("vector", lambda e: e.memset(kvb_sb[:, :, :], 0.0), w=[kkvb], tag="kvz")
    for h in range(16):
        w64 = (h % 2) * 64
        b.op("scalar", lambda e, h=h, w64=w64: e.mul(
                 out=kvb_sb[w64:w64 + 64, h // 2, w64:w64 + 64],
                 in_=kv_ps[w64:w64 + 64, h * 64:(h + 1) * 64],
                 mul=fbc[w64:w64 + 64, h:h + 1]),
             r=[kkv, kfb], w=[kkvb], tag=f"kve{h}")

    for g in range(NGl):
        b.load_group(x_h, g, xg, kxg, tag=f"ldxB{g}")
        for pr in range(8):
            b.op("tensor", lambda e, pr=pr, g=g: e.matmul(
                     at_ps[:, :], kvb_sb[:, pr, :],
                     qhT[:, pr, g * 512:(g + 1) * 512],
                     start=True, stop=True), r=[kkvb, kqhT], w=[kat], tag=f"at{pr}")
            b.op("scalar", lambda e, pr=pr: e.copy(out=atb[:, pr, :], in_=at_ps[:, :]),
                 r=[kat], w=[katb])
        wT, kwT = wTs["Wo"]
        for tt in range(4):
            for m in range(2):
                for k in range(8):
                    b.op("tensor", lambda e, m=m, k=k, tt=tt, wT=wT: e.matmul(
                             o_ps[:, m * 512:(m + 1) * 512], atb[:, k, tt * 128:(tt + 1) * 128],
                             wT[:, k, m * 512:(m + 1) * 512], start=(k == 0), stop=(k == 7)),
                         r=[katb, kwT], w=[kops])
            b.op("vector", lambda e, tt=tt: e.tensor_add(xg[:, tt, :], o_ps[:, :], xg[:, tt, :]),
                 r=[kops, kxg], w=[kxg])
        b.store_group(xo_h, g, xg, kxg, tag=f"stxB{g}")
    return b


def build_l1attn(T=4096):
    b = B("l1attn", T)
    NGl = T // 512
    x_h = b.inp("x", (T, D))
    wqkv_h = b.inp("Wqkv", (3072, D))
    bqkv_h = b.inp("bqkv", (3072,))
    wo_h = b.inp("Wo", (D, D))
    bo_h = b.inp("bo", (D,))
    xo_h = b.outp("xo", (T, D))
    ones, kon = b.const("ones_row")
    ident, kid = b.const("ident")
    m128c, km1 = b.const("m128c")
    m256, km2 = b.const("m256")
    m384a, km3 = b.const("m384a")
    masks = {1: (m128c, km1), 2: (m256, km2), 3: (m384a, km3)}

    wldb, kwldb = b.scratch("wldb", (128, D), BF16)
    wqkvT = b.sb("wqkvT", (128, 8, 3072), BF16); kwqkvT = ("wqkvT",)
    woT = b.sb("woT", (128, 8, D), BF16); kwoT = ("woT",)
    for (w_h, wT, kwT, nrc) in ((wqkv_h, wqkvT, kwqkvT, 24), (wo_h, woT, kwoT, 8)):
        for rc in range(nrc):
            b.cast_load(w_h[rc * 128:(rc + 1) * 128, :], wldb[:, :], kwldb, tag=f"wl{rc}")
            b.transposes("trA", lambda c: wldb[:, c * 128:(c + 1) * 128], 8, [kwldb],
                         lambda wT=wT, rc=rc: wT[:, :, rc * 128:(rc + 1) * 128], kwT)
    bqr, kbqr = b.scratch("bqr", (1, 3072), BF16)
    bor, kbor = b.scratch("bor", (1, D), BF16)
    b.cast_load(bqkv_h.rearrange("(a n) -> a n", a=1), bqr[:, :], kbqr, tag="lbq")
    b.cast_load(bo_h.rearrange("(a n) -> a n", a=1), bor[:, :], kbor, tag="lbo")

    xg, kxg = b.scratch("xg", (128, 4, D))
    xnb, kxnb = b.scratch("xnb", (128, D), BF16)
    xnTg, kxnT = b.scratch("xnTg", (128, 8, 512), BF16)
    qTg, kqT = b.scratch("qTg", (128, 8, 512), BF16)
    kTr, kkTr = b.scratch("kTr", (128, 8, 2, 512), BF16)
    vring, kvr = b.scratch("vring", (128, 6, 1024), BF16)
    Pb, kPb = b.scratch("Pb", (128, 384), BF16)
    PTb, kPTb = b.scratch("PTb", (128, 384), BF16)
    atb, katb = b.scratch("atb", (128, 8, 512), BF16)
    mx, kmx = b.scratch("mx", (128, 1))
    bias, kbias = b.scratch("biasx", (128, 1))
    rs, krs = b.scratch("rs", (128, 1))
    dg, kdg = b.scratch("dg", (128, 128), BF16)
    p_ps, kpps = b.scratch("p_ps", (128, 512), psum=True)
    s_ps, ksps = b.scratch("s_ps", (128, 384), psum=True)
    pt_ps, kptps = b.scratch("pt_ps", (128, 384), psum=True)
    av_ps, kavps = b.scratch("av_ps", (128, 128), psum=True)
    o_ps, kops = b.scratch("o_ps", (128, D), psum=True)

    for g in range(NGl):
        b.load_group(x_h, g, xg, kxg, tag=f"ldx{g}")
        for tt in range(4):
            xt = xg[:, tt, :]
            b.ln_tile("ln", ((xt[:, 0:512], kxg), (xt[:, 512:1024], kxg)), xnb[:, :], kxnb)
            b.transposes("trA", lambda c: xnb[:, c * 128:(c + 1) * 128], 8, [kxnb],
                         lambda tt=tt: xnTg[:, :, tt * 128:(tt + 1) * 128], kxnT)
        for (dstT, kdT, base, slot) in ((qTg, kqT, 0, None), (kTr, kkTr, 1024, g % 2)):
            for qc in range(8):
                r0 = base + qc * 128
                for k in range(8):
                    b.op("tensor", lambda e, qc=qc, k=k, r0=r0: e.matmul(
                             p_ps[:, :], wqkvT[:, k, r0:r0 + 128], xnTg[:, k, :],
                             start=(k == 0), stop=False), r=[kwqkvT, kxnT], w=[kpps])
                b.op("tensor", lambda e, r0=r0: e.matmul(
                         p_ps[:, :], bqr[0:1, r0:r0 + 128], ones[0:1, 0:512],
                         start=False, stop=True), r=[kbqr, kon], w=[kpps])
                if slot is None:
                    b.op("scalar", lambda e, qc=qc: e.copy(out=qTg[:, qc, :], in_=p_ps[:, :]),
                         r=[kpps], w=[kqT])
                else:
                    b.op("scalar", lambda e, qc=qc, slot=slot: e.copy(out=kTr[:, qc, slot, :], in_=p_ps[:, :]),
                         r=[kpps], w=[kkTr])
        for tt in range(4):
            slot = (g * 4 + tt) % 6
            for vc in range(2):
                for k in range(8):
                    b.op("tensor", lambda e, tt=tt, vc=vc, k=k: e.matmul(
                             p_ps[:, :], xnTg[:, k, tt * 128:(tt + 1) * 128],
                             wqkvT[:, k, 2048 + vc * 512:2048 + (vc + 1) * 512],
                             start=(k == 0), stop=False), r=[kxnT, kwqkvT], w=[kpps])
                b.op("tensor", lambda e, vc=vc: e.matmul(
                         p_ps[:, :], ones[0:1, 0:128], bqr[0:1, 2048 + vc * 512:2048 + (vc + 1) * 512],
                         start=False, stop=True), r=[kon, kbqr], w=[kpps])
                b.op("scalar", lambda e, slot=slot, vc=vc: e.copy(
                         out=vring[:, slot, vc * 512:(vc + 1) * 512], in_=p_ps[:, :]),
                     r=[kpps], w=[(f"vr{slot}",)])
        for tt in range(4):
            qt = g * 4 + tt
            nkc = min(3, qt + 1)
            W = nkc * 128
            mk_t, mk_k = masks[nkc]
            kts = [qt - nkc + 1 + i for i in range(nkc)]
            for h in range(16):
                p0 = (h % 2) * 64
                qsl = qTg[p0:p0 + 64, h // 2, tt * 128:(tt + 1) * 128]
                for i, kt in enumerate(kts):
                    ksl = kTr[p0:p0 + 64, h // 2, (kt // 4) % 2, (kt % 4) * 128:(kt % 4) * 128 + 128]
                    b.op("tensor", lambda e, i=i, ksl=ksl, qsl=qsl: e.matmul(
                             s_ps[:, i * 128:(i + 1) * 128], qsl, ksl,
                             start=(i == 0), stop=False), r=[kqT, kkTr], w=[ksps])
                b.op("tensor", lambda e, W=W, mk_t=mk_t: e.matmul(
                         s_ps[:, 0:W], ident[:, :], mk_t[:, 0:W],
                         start=False, stop=True), r=[kid, mk_k], w=[ksps])
                b.op("vector", lambda e, W=W: e.reduce_max(out=mx[:, :], in_=s_ps[:, 0:W], axis=AX.X),
                     r=[ksps], w=[kmx])
                b.op("vector", lambda e: e.tensor_scalar(out=bias[:, :], in0=mx[:, :],
                         scalar1=-0.125, scalar2=None, op0=ALU.mult), r=[kmx], w=[kbias])
                b.op("scalar", lambda e, W=W: e.activation(out=Pb[:, 0:W], in_=s_ps[:, 0:W],
                         func=AF.Exp, bias=bias[:, :], scale=0.125, accum_out=rs[:, :]),
                     r=[ksps, kbias], w=[kPb, krs])
                b.op("vector", lambda e: e.reciprocal(out=rs[:, :], in_=rs[:, :]), r=[krs], w=[krs])
                b.op("vector", lambda e: e.tensor_scalar(out=dg[:, :], in0=ident[:, :],
                         scalar1=rs[:, :], scalar2=None, op0=ALU.mult), r=[krs, kid], w=[kdg])
                for i in range(nkc):
                    b.op("tensor", lambda e, i=i, nkc=nkc: e.matmul(
                             pt_ps[:, i * 128:(i + 1) * 128], Pb[:, i * 128:(i + 1) * 128], dg[:, :],
                             start=(i == 0), stop=(i == nkc - 1)), r=[kPb, kdg], w=[kptps])
                b.op("scalar", lambda e, W=W: e.copy(out=PTb[:, 0:W], in_=pt_ps[:, 0:W]),
                     r=[kptps], w=[kPTb])
                for i, kt in enumerate(kts):
                    vsl = vring[:, kt % 6, h * 64:(h + 1) * 64]
                    b.op("tensor", lambda e, i=i, vsl=vsl, p0=p0: e.matmul(
                             av_ps[p0:p0 + 64, :], vsl, PTb[:, i * 128:(i + 1) * 128],
                             start=(i == 0), stop=(i == nkc - 1), skip_group_check=True),
                         r=[(f"vr{kt % 6}",), kPTb], w=[kavps])
                if h % 2 == 1:
                    b.op("scalar", lambda e, h=h, tt=tt: e.copy(
                             out=atb[:, h // 2, tt * 128:(tt + 1) * 128], in_=av_ps[:, :]),
                         r=[kavps], w=[katb])
        for tt in range(4):
            for m in range(2):
                for c in range(8):
                    b.op("tensor", lambda e, m=m, c=c, tt=tt: e.matmul(
                             o_ps[:, m * 512:(m + 1) * 512], atb[:, c, tt * 128:(tt + 1) * 128],
                             woT[:, c, m * 512:(m + 1) * 512], start=(c == 0), stop=False),
                         r=[katb, kwoT], w=[kops])
                b.op("tensor", lambda e, m=m: e.matmul(
                         o_ps[:, m * 512:(m + 1) * 512], ones[0:1, 0:128], bor[0:1, m * 512:(m + 1) * 512],
                         start=False, stop=True), r=[kon, kbor], w=[kops])
            b.op("vector", lambda e, tt=tt: e.tensor_add(xg[:, tt, :], o_ps[:, :], xg[:, tt, :]),
                 r=[kops, kxg], w=[kxg])
        b.store_group(xo_h, g, xg, kxg, tag=f"st{g}")
    return b


def build_l1init(T=4096):
    b = B("l1init", T)
    x_h = b.inp("x", (T, D))
    y_h = b.outp("y", (T, D))
    xg, kxg = b.scratch("xg", (128, 4, D))
    yg, kyg = b.scratch("yg", (128, 4, D))
    for g in range(T // 512):
        b.load_group(x_h, g, xg, kxg, tag=f"l{g}")
        for tt in range(4):
            xt = xg[:, tt, :]
            b.ln_tile("ln", ((xt[:, 0:512], kxg), (xt[:, 512:1024], kxg)), yg[:, tt, :], kyg)
        b.store_group(y_h, g, yg, kyg, tag=f"s{g}")
    return b


def build_l1fin(T=4096):
    b = B("l1fin", T)
    x_h = b.inp("x", (T, D))
    y_h = b.inp("y", (T, D))
    xo_h = b.outp("xo", (T, D))
    xg, kxg = b.scratch("xg", (128, 4, D))
    yg, kyg = b.scratch("yg", (128, 4, D))
    for g in range(T // 512):
        b.load_group(x_h, g, xg, kxg, tag=f"lx{g}")
        b.load_group(y_h, g, yg, kyg, tag=f"ly{g}")
        for tt in range(4):
            b.op("vector", lambda e, tt=tt: e.scalar_tensor_tensor(
                     out=xg[:, tt, :], in0=yg[:, tt, :], scalar=0.5, in1=xg[:, tt, :],
                     op0=ALU.mult, op1=ALU.add), r=[kxg, kyg], w=[kxg])
        b.store_group(xo_h, g, xg, kxg, tag=f"sx{g}")
    return b


def build_l1iter(T=4096):
    b = B("l1iter", T)
    NGl = T // 512
    y_h = b.inp("y", (T, D))
    wqkv_h = b.inp("Wqkv", (D, 3072))
    qs_h = b.inp("qs", (64,))
    ks_h = b.inp("ks", (64,))
    wo_h = b.inp("Wo_", (D, D))
    g_h = b.inp("g_ln", (D,))
    bln_h = b.inp("b_ln", (D,))
    w1_h = b.inp("W1", (D, 5460))
    b1_h = b.inp("b1", (5460,))
    w2_h = b.inp("W2", (2730, D))
    b2_h = b.inp("b2", (D,))
    yo_h = b.outp("yo", (T, D))
    y2_h = b.nc.dram_tensor("y2scr", [T, D], F32, kind="Internal").ap()

    ones, kon = b.const("ones_row")
    ident, kid = b.const("ident")
    m128c, km1 = b.const("m128c")
    m256, km2 = b.const("m256")
    m384f, km3 = b.const("m384f")
    masks = {1: (m128c, km1), 2: (m256, km2), 3: (m384f, km3)}
    rotA = {}
    for n in ("rotAq", "rotBq", "rotAk", "rotBk"):
        arr = b.consts[n]
        h = b.inp("cst_" + n, arr.shape, BF16, const=arr)
        t = b.sb("c_" + n, (128, 4, 64), BF16)
        rotA[n] = (t, ("c_" + n,), h)

    def load_rot_group(g):
        for n in ("rotAq", "rotBq", "rotAk", "rotBk"):
            t, key, h = rotA[n]
            b.dma("sync", lambda e, t=t, h=h, g=g: e.dma_start(
                      out=t[:, :, :],
                      in_=h.rearrange("(n p) d -> p n d", p=128)[:, g * 4:(g + 1) * 4, :]),
                  w=[key], tag=f"ld{n}{g}")

    wqkv = b.sb("wqkv", (128, 8, 3072), BF16); kwqkv = ("wqkv",)
    wo = b.sb("wo", (128, 8, D), BF16); kwo = ("wo",)
    b.cast_load(wqkv_h.rearrange("(kc p) n -> p kc n", p=128), wqkv[:, :, :], kwqkv, tag="lwqkv")
    b.cast_load(wo_h.rearrange("(kc p) n -> p kc n", p=128), wo[:, :, :], kwo, tag="lwo")
    qsr_ps, kqsr_ps = b.scratch("qsr_ps", (128, 64), psum=True)
    onesf, konf = b.const("ones_f32")
    reps = {}
    for nm, hh in (("qs", qs_h), ("ks", ks_h)):
        row, krow = b.scratch(nm + "row", (1, 64))
        b.dma("sync", lambda e, row=row, hh=hh: e.dma_start(out=row[:, :], in_=hh.rearrange("(a n) -> a n", a=1)),
              w=[krow], tag="l" + nm)
        rep, krep = b.scratch(nm + "rep", (128, 64))
        b.op("tensor", lambda e, row=row: e.matmul(qsr_ps[:, :], onesf[:, :], row[:, :], start=True, stop=True),
             r=[krow, konf], w=[kqsr_ps])
        b.op("scalar", lambda e, rep=rep: e.copy(out=rep[:, :], in_=qsr_ps[:, :]), r=[kqsr_ps], w=[krep])
        reps[nm] = (rep, krep)

    yt, kyt = b.scratch("yt", (128, D))
    zf, kzf = b.scratch("zf", (128, D))
    zg, kzg = b.scratch("zg", (128, 4, D), BF16)
    znT, kznT = b.scratch("znT", (128, 8, 128), BF16)
    q_sb, kq = b.scratch("q_sb", (128, D))
    k_sb, kk = b.scratch("k_sb", (128, D))

    ssq, kssq = b.scratch("ssq", (128, 16))
    rstd, krst = b.scratch("rstdh", (128, 16))
    eq, keq = b.scratch("eq", (128, 16))
    qtb, kqtb = b.scratch("qtb", (128, 16, 128), BF16)
    ktb, kktb = b.scratch("ktb", (128, 16, 128), BF16)
    qpb, kqpb = b.scratch("qpb", (128, 16, 64), BF16)
    kpb, kkpb = b.scratch("kpb", (128, 16, 64), BF16)
    qTt, kqTt = b.scratch("qTt", (128, 8, 128), BF16)
    kTr, kkTr = b.scratch("kTr", (128, 8, 4, 128), BF16)
    vring, kvr = b.scratch("vring", (128, 4, 1024), BF16)
    aq, kaq = b.scratch("aq", (128, 64), BF16)
    bq, kbq = b.scratch("bq", (128, 64), BF16)
    ak, kak = b.scratch("ak", (128, 64), BF16)
    bk, kbk = b.scratch("bk", (128, 64), BF16)
    Pb, kPb = b.scratch("Pb", (128, 384), BF16)
    PTb, kPTb = b.scratch("PTb", (128, 384), BF16)
    atb, katb = b.scratch("atb", (128, 8, 512), BF16)
    mx, kmx = b.scratch("mx", (128, 1))
    bias, kbias = b.scratch("biasx", (128, 1))
    rs, krs = b.scratch("rs", (128, 1))
    dg, kdg = b.scratch("dg", (128, 128), BF16)
    p_ps, kpps = b.scratch("p_ps", (128, 512), psum=True)
    s_ps, ksps = b.scratch("s_ps", (128, 384), psum=True)
    pt_ps, kptps = b.scratch("pt_ps", (128, 512), psum=True)
    av_ps, kavps = b.scratch("av_ps", (128, 128), psum=True)
    o_ps, kops = b.scratch("o_ps", (128, D), psum=True)

    sqv = ktb[:, :, :].rearrange("p a b -> p (a b)")[:, 0:D]
    def l2n_stats(src, ksrc, with8):
        b.op("vector", lambda e: e.tensor_mul(sqv, src[:, :], src[:, :]), r=[ksrc], w=[kktb])
        b.op("vector", lambda e: e.tensor_reduce(out=ssq[:, :],
                in_=sqv.rearrange("p (h d) -> p h d", d=64), axis=AX.X, op=ALU.add),
             r=[kktb], w=[kssq])
        b.op("vector", lambda e: e.tensor_scalar(out=ssq[:, :], in0=ssq[:, :], scalar1=1e-24,
                scalar2=None, op0=ALU.max), r=[kssq], w=[kssq])
        b.op("scalar", lambda e: e.activation(out=rstd[:, :], in_=ssq[:, :], func=AF.Ln), r=[kssq], w=[krst])
        b.op("scalar", lambda e: e.activation(out=rstd[:, :], in_=rstd[:, :], func=AF.Exp, scale=-0.5),
             r=[krst], w=[krst])
        if with8:
            b.op("vector", lambda e: e.tensor_scalar(out=eq[:, :], in0=rstd[:, :], scalar1=8.0,
                    scalar2=None, op0=ALU.mult), r=[krst], w=[keq])

    for g in range(NGl):
        load_rot_group(g)
        for tt in range(4):
            r0 = g * 512 + tt * 128
            b.dma("sync", lambda e, r0=r0: e.dma_start(out=zf[:, 0:512], in_=y_h[r0:r0 + 128, 0:512]),
                  w=[kzf], tag=f"z1_{g}_{tt}")
            if r0 == 0:
                b.op("vector", lambda e: e.memset(zf[0:1, 512:1024], 0.0), w=[kzf], tag="z0z")
                b.dma("sync", lambda e: e.dma_start(out=zf[1:128, 512:1024], in_=y_h[0:127, 512:1024]),
                      w=[kzf], tag="z2a")
            else:
                b.dma("sync", lambda e, r0=r0: e.dma_start(out=zf[:, 512:1024], in_=y_h[r0 - 1:r0 + 127, 512:1024]),
                      w=[kzf], tag=f"z2_{g}_{tt}")
            b.op("gpsimd", lambda e, tt=tt: e.tensor_copy(out=zg[:, tt, :], in_=zf[:, :]),
                 r=[kzf], w=[kzg], tag=f"zc{g}{tt}")
        for tt in range(4):
            ti = g * 4 + tt
            qt = ti
            b.transposes("trA", lambda c, tt=tt: zg[:, tt, c * 128:(c + 1) * 128], 8, [kzg],
                         lambda: znT[:, :, :], kznT)
            for (dst, kdst, base, isv) in ((q_sb, kq, 0, False), (k_sb, kk, 1024, False), (None, None, 2048, True)):
                for m in range(2):
                    for k in range(8):
                        b.op("tensor", lambda e, m=m, k=k, base=base: e.matmul(
                                 p_ps[:, :], znT[:, k, :], wqkv[:, k, base + m * 512:base + (m + 1) * 512],
                                 start=(k == 0), stop=(k == 7)), r=[kznT, kwqkv], w=[kpps])
                    if isv:
                        b.op("scalar", lambda e, m=m, ti=ti: e.copy(
                                 out=vring[:, ti % 4, m * 512:(m + 1) * 512], in_=p_ps[:, :]),
                             r=[kpps], w=[(f"vr{ti % 4}",)])
                    else:
                        b.op("scalar", lambda e, dst=dst, m=m: e.copy(out=dst[:, m * 512:(m + 1) * 512],
                                 in_=p_ps[:, :]), r=[kpps], w=[kdst])
            qsrep, kqsrep = reps["qs"]
            ksrep, kksrep = reps["ks"]
            v3 = lambda t2: t2.rearrange("p (r s) -> p r s", r=2)
            sig = lambda t2: v3(t2[:, :])[:, ::-1, :]
            b.op("vector", lambda e, tt=tt: e.tensor_mul(aq[:, :], rotA["rotAq"][0][:, tt, :], qsrep[:, :]),
                 r=[rotA["rotAq"][1], kqsrep], w=[kaq])
            b.op("vector", lambda e, tt=tt: e.tensor_mul(v3(bq[:, :]), v3(rotA["rotBq"][0][:, tt, :]), sig(qsrep)),
                 r=[rotA["rotBq"][1], kqsrep], w=[kbq])
            b.op("vector", lambda e, tt=tt: e.tensor_mul(ak[:, :], rotA["rotAk"][0][:, tt, :], ksrep[:, :]),
                 r=[rotA["rotAk"][1], kksrep], w=[kak])
            b.op("vector", lambda e, tt=tt: e.tensor_mul(v3(bk[:, :]), v3(rotA["rotBk"][0][:, tt, :]), sig(ksrep)),
                 r=[rotA["rotBk"][1], kksrep], w=[kbk])
            l2n_stats(q_sb, kq, with8=True)
            qv = q_sb[:, :].rearrange("p (h d) -> p h d", d=64)
            qsg = q_sb[:, :].rearrange("p (h r s) -> p h r s", r=2, s=32)[:, :, ::-1, :]
            b.op("vector", lambda e: e.tensor_mul(qtb[:, :, 0:64], qv,
                     aq[:, :].rearrange("p (a d) -> p a d", a=1).broadcast_to((128, 16, 64))),
                 r=[kq, kaq], w=[kqtb])
            b.op("vector", lambda e: e.tensor_mul(
                     qtb[:, :, 64:128].rearrange("p h (r s) -> p h r s", r=2), qsg,
                     bq[:, :].rearrange("p (a r s) -> p a r s", a=1, r=2).broadcast_to((128, 16, 2, 32))),
                 r=[kq, kbq], w=[kqtb])
            l2n_stats(k_sb, kk, with8=False)
            b.op("vector", lambda e: e.tensor_mul(k_sb[:, :].rearrange("p (h d) -> p h d", d=64),
                     k_sb[:, :].rearrange("p (h d) -> p h d", d=64),
                     rstd[:, :].broadcast_to((128, 16, 64))), r=[kk, krst], w=[kk])
            kv2 = k_sb[:, :].rearrange("p (h d) -> p h d", d=64)
            ksg2 = k_sb[:, :].rearrange("p (h r s) -> p h r s", r=2, s=32)[:, :, ::-1, :]
            b.op("vector", lambda e: e.tensor_mul(ktb[:, :, 0:64], kv2,
                     ak[:, :].rearrange("p (a d) -> p a d", a=1).broadcast_to((128, 16, 64))),
                 r=[kk, kak], w=[kktb])
            b.op("vector", lambda e: e.tensor_mul(
                     ktb[:, :, 64:128].rearrange("p h (r s) -> p h r s", r=2), ksg2,
                     bk[:, :].rearrange("p (a r s) -> p a r s", a=1, r=2).broadcast_to((128, 16, 2, 32))),
                 r=[kk, kbk], w=[kktb])
            b.op("vector", lambda e: e.tensor_add(qpb[:, :, :], qtb[:, :, 0:64], qtb[:, :, 64:128]),
                 r=[kqtb], w=[kqpb])
            b.op("vector", lambda e: e.tensor_add(kpb[:, :, :], ktb[:, :, 0:64], ktb[:, :, 64:128]),
                 r=[kktb], w=[kkpb])
            b.transposes("trA", lambda c: qpb[:, 2 * c:2 * c + 2, :].rearrange("p a b -> p (a b)"),
                         8, [kqpb], lambda: qTt[:, :, :], kqTt)
            b.transposes("trA", lambda c: kpb[:, 2 * c:2 * c + 2, :].rearrange("p a b -> p (a b)"),
                         8, [kkpb], lambda ti=ti: kTr[:, :, ti % 4, :], (f"ktr{ti % 4}",))
            nkc = min(3, qt + 1)
            W = nkc * 128
            mk_t, mk_k = masks[nkc]
            kts = [qt - nkc + 1 + i for i in range(nkc)]
            for h in range(16):
                p0h = (h % 2) * 64
                for i, kt in enumerate(kts):
                    b.op("tensor", lambda e, i=i, h=h, kt=kt, p0h=p0h: e.matmul(
                             s_ps[:, i * 128:(i + 1) * 128], qTt[p0h:p0h + 64, h // 2, :],
                             kTr[p0h:p0h + 64, h // 2, kt % 4, :],
                             start=(i == 0), stop=False), r=[kqTt, (f"ktr{kt % 4}",)], w=[ksps])
                b.op("tensor", lambda e, W=W, mk_t=mk_t: e.matmul(
                         s_ps[:, 0:W], ident[:, :], mk_t[:, 0:W],
                         start=False, stop=True), r=[kid, mk_k], w=[ksps])
                b.op("vector", lambda e, W=W: e.reduce_max(out=mx[:, :], in_=s_ps[:, 0:W], axis=AX.X),
                     r=[ksps], w=[kmx])
                b.op("vector", lambda e, h=h: e.tensor_scalar(out=bias[:, :], in0=mx[:, :],
                         scalar1=eq[:, h:h + 1], scalar2=-1.0, op0=ALU.mult, op1=ALU.mult),
                     r=[kmx, keq], w=[kbias])
                b.op("scalar", lambda e, W=W, h=h: e.activation(out=Pb[:, 0:W], in_=s_ps[:, 0:W],
                         func=AF.Exp, bias=bias[:, :], scale=eq[:, h:h + 1], accum_out=rs[:, :]),
                     r=[ksps, kbias, keq], w=[kPb, krs])
                b.op("vector", lambda e: e.reciprocal(out=rs[:, :], in_=rs[:, :]), r=[krs], w=[krs])
                b.op("vector", lambda e: e.tensor_scalar(out=dg[:, :], in0=ident[:, :],
                         scalar1=rs[:, :], scalar2=None, op0=ALU.mult), r=[krs, kid], w=[kdg])
                for i in range(nkc):
                    b.op("tensor", lambda e, i=i, nkc=nkc: e.matmul(
                             pt_ps[:, i * 128:(i + 1) * 128], Pb[:, i * 128:(i + 1) * 128], dg[:, :],
                             start=(i == 0), stop=(i == nkc - 1)), r=[kPb, kdg], w=[kptps])
                b.op("scalar", lambda e, W=W: e.copy(out=PTb[:, 0:W], in_=pt_ps[:, 0:W]),
                     r=[kptps], w=[kPTb])
                p0 = (h % 2) * 64
                for i, kt in enumerate(kts):
                    b.op("tensor", lambda e, i=i, kt=kt, h=h, p0=p0, nkc=nkc: e.matmul(
                             av_ps[p0:p0 + 64, :], vring[:, kt % 4, h * 64:(h + 1) * 64],
                             PTb[:, i * 128:(i + 1) * 128],
                             start=(i == 0), stop=(i == nkc - 1), skip_group_check=True),
                         r=[(f"vr{kt % 4}",), kPTb], w=[kavps])
                if h % 2 == 1:
                    b.op("scalar", lambda e, h=h, tt=tt: e.copy(
                             out=atb[:, h // 2, tt * 128:(tt + 1) * 128], in_=av_ps[:, :]),
                         r=[kavps], w=[katb])
                import os as _os
                if _os.environ.get("MKDEBUG") and g == 0 and tt == 0 and h == 0:
                    dqT_h = b.outp("dbg_qTt", (128, 8, 128), BF16)
                    b.dma("sync", lambda e: e.dma_start(out=dqT_h[...], in_=qTt[:, :, :]), r=[kqTt], tag="dqT")
                    dkT_h = b.outp("dbg_kT0", (128, 8, 128), BF16)
                    b.dma("sync", lambda e: e.dma_start(out=dkT_h[...], in_=kTr[:, :, 0, :]), r=[("ktr0",)], tag="dkT")
                    dS_h = b.outp("dbg_S0", (128, 384))
                    dSs, kdSs = b.scratch("dbgSs", (128, 384))
                    b.op("vector", lambda e: e.memset(dSs[:, :], 0.0), w=[kdSs])
                    b.op("vector", lambda e, W=W: e.tensor_copy(out=dSs[:, 0:W], in_=s_ps[:, 0:W]),
                         r=[ksps], w=[kdSs])
                    b.dma("sync", lambda e: e.dma_start(out=dS_h[...], in_=dSs[:, :]), r=[kdSs], tag="dS0")
                    dP_h = b.outp("dbg_P0", (128, 384), BF16)
                    b.op("vector", lambda e, W=W: e.memset(Pb[:, W:384], 0.0), w=[kPb])
                    b.dma("sync", lambda e: e.dma_start(out=dP_h[...], in_=Pb[:, :]), r=[kPb], tag="dP0")
                    dA_h = b.outp("dbg_av0", (128, 128))
                    dAs, kdAs = b.scratch("dbgAs", (128, 128))
                    b.op("vector", lambda e: e.memset(dAs[:, :], 0.0), w=[kdAs])
                    b.op("vector", lambda e: e.tensor_copy(out=dAs[0:64, :], in_=av_ps[0:64, :]),
                         r=[kavps], w=[kdAs])
                    b.dma("sync", lambda e: e.dma_start(out=dA_h[...], in_=dAs[:, :]), r=[kdAs], tag="dA0")
        for tt in range(4):
            r0 = g * 512 + tt * 128
            b.dma("sync", lambda e, r0=r0: e.dma_start(out=yt[:, :], in_=y_h[r0:r0 + 128, :]),
                  w=[kyt], tag=f"alyr{g}{tt}")
            for m in range(2):
                for c in range(8):
                    b.op("tensor", lambda e, m=m, c=c, tt=tt: e.matmul(
                             o_ps[:, m * 512:(m + 1) * 512], atb[:, c, tt * 128:(tt + 1) * 128],
                             wo[:, c, m * 512:(m + 1) * 512], start=(c == 0), stop=(c == 7)),
                         r=[katb, kwo], w=[kops])
            b.op("vector", lambda e: e.tensor_add(yt[:, :], o_ps[:, :], yt[:, :]),
                 r=[kops, kyt], w=[kyt])
            b.dma("sync", lambda e, r0=r0: e.dma_start(out=y2_h[r0:r0 + 128, :], in_=yt[:, :]),
                  r=[kyt], w=[("y2h", g)], tag=f"ast{g}{tt}")

    import os
    if os.environ.get("MKDEBUG"):
        y2o_h = b.outp("dbg_y2", (T, D))
        for g in range(NGl):
            for tt in range(4):
                r0 = g * 512 + tt * 128
                b.dma("sync", lambda e, r0=r0: e.dma_start(out=zf[:, :], in_=y2_h[r0:r0 + 128, :]),
                      r=[("y2h", g)], w=[kzf], tag=f"dbgl{r0}")
                b.dma("sync", lambda e, r0=r0: e.dma_start(out=y2o_h[r0:r0 + 128, :], in_=zf[:, :]),
                      r=[kzf], tag=f"dbgs{r0}")
        qtb_h = b.outp("dbg_qtb", (128, 16, 128), BF16)
        b.dma("sync", lambda e: e.dma_start(out=qtb_h[...], in_=qtb[:, :, :]), r=[kqtb], tag="dbgqtb")
        ktb_h = b.outp("dbg_ktb", (128, 16, 128), BF16)
        b.dma("sync", lambda e: e.dma_start(out=ktb_h[...], in_=ktb[:, :, :]), r=[kktb], tag="dbgktb")
    w1a = [b.scratch(f"w1a{i}", (128, 8, 128), BF16) for i in range(2)]
    w1g = [b.scratch(f"w1g{i}", (128, 8, 128), BF16) for i in range(2)]
    w1f, kw1f = b.scratch("w1f", (128, 8, 128))
    w2sb = wqkv[:, :, :].rearrange("p a b -> p (a b)")[:, 0:22528].rearrange("p (c n) -> p c n", n=D)
    kw2 = kwqkv
    h_sb, kh = b.scratch("h_sb", (128, 22, 512), BF16)
    a_sb, ka = b.scratch("a_sb", (128, 512), BF16)
    gl_sb, kgl = b.scratch("gl_sb", (128, 512), BF16)
    z2n, kz2n = b.scratch("z2n", (128, D), BF16)
    z2nT, kz2nT = b.scratch("z2nT", (128, 8, 512), BF16)
    btrow, kbt = b.scratch("btrow", (1, 5632), BF16)
    gcols, kgc = b.scratch("gcols", (128, 8))
    b2row, kb2r = b.scratch("b2row", (1, D), BF16)
    brow = z2n[0:1, :]
    kbrow = kz2n
    grow = z2n[0:1, :]
    kgrow = kz2n

    b.op("vector", lambda e: e.memset(w2sb[:, 21, :], 0.0), w=[kw2], tag="w2z")
    b.cast_load(w2_h[0:2688, :].rearrange("(kc p) n -> p kc n", p=128), w2sb[:, 0:21, :], kw2, tag="lw2")
    b.cast_load(w2_h[2688:2730, :].rearrange("(a p) n -> p a n", p=42), w2sb[0:42, 21:22, :], kw2, tag="lw2t")
    b.cast_load(b2_h.rearrange("(a n) -> a n", a=1), b2row[:, :], kb2r, tag="lb2")
    bcols, kbc = b.scratch("bcols", (128, 8), BF16)
    b.cast_load(g_h.rearrange("(a n) -> a n", a=1), grow[:, :], kgrow, tag="lgln")
    for kc in range(8):
        b.op("tensor", lambda e, kc=kc: e.matmul(p_ps[:, kc:kc + 1], grow[0:1, kc * 128:(kc + 1) * 128],
                 ones[0:1, 0:1], start=(kc == 0), stop=(kc == 7)), r=[kgrow, kon], w=[kpps])
    b.op("scalar", lambda e: e.copy(out=gcols[:, :], in_=p_ps[:, 0:8]), r=[kpps], w=[kgc])
    b.cast_load(bln_h.rearrange("(a n) -> a n", a=1), brow[:, :], kbrow, tag="lbln")
    for kc in range(8):
        b.op("tensor", lambda e, kc=kc: e.matmul(p_ps[:, kc:kc + 1], brow[0:1, kc * 128:(kc + 1) * 128],
                 ones[0:1, 0:1], start=(kc == 0), stop=(kc == 7)), r=[kbrow, kon], w=[kpps])
    b.op("scalar", lambda e: e.copy(out=bcols[:, :], in_=p_ps[:, 0:8]), r=[kpps], w=[kbc])
    b.op("vector", lambda e: e.memset(btrow[:, :], 0.0), w=[kbt], tag="btz")
    b.cast_load(b1_h[0:2730].rearrange("(a n) -> a n", a=1), btrow[0:1, 0:2730], kbt, tag="lb1a")
    b.cast_load(b1_h[2730:5460].rearrange("(a n) -> a n", a=1), btrow[0:1, 2816:5546], kbt, tag="lb1b")

    def w1_load(c, par, bterm=False):
        wa, kwa = w1a[par]
        wg, kwg = w1g[par]
        a0, a1 = c * 128, min(2730, (c + 1) * 128)
        g0, g1 = 2730 + c * 128, min(5460, 2730 + (c + 1) * 128)
        for (wt, kwt, c0, c1) in ((wa, kwa, a0, a1), (wg, kwg, g0, g1)):
            if c1 - c0 < 128:
                b.op("vector", lambda e, wt=wt: e.memset(wt[:, :, :], 0.0), w=[kwt], tag=f"wz{c}")
            b.dma("sync", lambda e, c0=c0, c1=c1: e.dma_start(
                      out=w1f[:, :, 0:c1 - c0],
                      in_=w1_h[:, c0:c1].rearrange("(kc p) n -> p kc n", p=128)),
                  w=[kw1f], tag=f"w1l{c}")
            b.op("gpsimd", lambda e, wt=wt, c0=c0, c1=c1: e.tensor_copy(
                     out=wt[:, :, 0:c1 - c0], in_=w1f[:, :, 0:c1 - c0]), r=[kw1f], w=[kwt], tag=f"w1c{c}")
        if bterm:
            for (wt, kwt, off) in ((wa, kwa, c * 128), (wg, kwg, 2816 + c * 128)):
                for kc in range(8):
                    b.op("tensor", lambda e, wt=wt, kc=kc: e.matmul(
                             s_ps[0:1, 0:128], bcols[:, kc:kc + 1], wt[:, kc, :],
                             start=(kc == 0), stop=(kc == 7)), r=[kbc, kwt], w=[ksps])
                b.op("vector", lambda e, off=off: e.tensor_add(btrow[0:1, off:off + 128],
                         btrow[0:1, off:off + 128], s_ps[0:1, 0:128]), r=[ksps, kbt], w=[kbt])
        for (wt, kwt) in ((wa, kwa), (wg, kwg)):
            for kc in range(8):
                b.op("vector", lambda e, wt=wt, kc=kc: e.tensor_scalar(
                         out=wt[:, kc, :], in0=wt[:, kc, :], scalar1=gcols[:, kc:kc + 1],
                         scalar2=None, op0=ALU.mult), r=[kwt, kgc], w=[kwt])

    for c in range(22):
        w1_load(c, c % 2, bterm=True)


    for g in range(NGl):
        for tt in range(4):
            r0 = g * 512 + tt * 128
            b.dma("sync", lambda e, r0=r0: e.dma_start(out=zf[:, 0:512], in_=y2_h[r0:r0 + 128, 0:512]),
                  r=[("y2h", g)] + ([("y2h", g - 1)] if g > 0 else []), w=[kzf], tag=f"z1_{g}_{tt}")
            if r0 == 0:
                b.op("vector", lambda e: e.memset(zf[0:1, 512:1024], 0.0), w=[kzf], tag="z0z")
                b.dma("sync", lambda e: e.dma_start(out=zf[1:128, 512:1024], in_=y2_h[0:127, 512:1024]),
                      r=[("y2h", g)] + ([("y2h", g - 1)] if g > 0 else []), w=[kzf], tag="z2a")
            else:
                b.dma("sync", lambda e, r0=r0: e.dma_start(out=zf[:, 512:1024], in_=y2_h[r0 - 1:r0 + 127, 512:1024]),
                      r=[("y2h", g)] + ([("y2h", g - 1)] if g > 0 else []), w=[kzf], tag=f"z2_{g}_{tt}")
            b.op("gpsimd", lambda e, tt=tt: e.tensor_copy(out=zg[:, tt, :], in_=zf[:, :]),
                 r=[kzf], w=[kzg], tag=f"zc{g}{tt}")

        for tt in range(4):
            zt = zg[:, tt, :]
            b.ln_tile("ln2", ((zt[:, 0:512], kzg), (zt[:, 512:1024], kzg)), z2n[:, :], kz2n)
            b.transposes("trA", lambda c: z2n[:, c * 128:(c + 1) * 128], 8, [kz2n],
                         lambda tt=tt: z2nT[:, :, tt * 128:(tt + 1) * 128], kz2nT)
        for c in range(22):
            par = c % 2
            w1_load(c, par)
            wa, kwa = w1a[par]
            wg, kwg = w1g[par]
            for (wt, kwt, ps, kps2, boff) in ((wa, kwa, p_ps, kpps, c * 128),
                                              (wg, kwg, pt_ps, kptps, 2816 + c * 128)):
                for k in range(8):
                    b.op("tensor", lambda e, wt=wt, k=k, ps=ps: e.matmul(
                             ps[:, 0:512], wt[:, k, :], z2nT[:, k, :],
                             start=(k == 0), stop=False), r=[kwt, kz2nT], w=[kps2])
                b.op("tensor", lambda e, ps=ps, boff=boff: e.matmul(
                         ps[:, 0:512], btrow[0:1, boff:boff + 128], ones[0:1, 0:512],
                         start=False, stop=True), r=[kbt, kon], w=[kps2])
            b.op("scalar", lambda e: e.copy(out=a_sb[:, :], in_=p_ps[:, 0:512]), r=[kpps], w=[ka])
            b.op("scalar", lambda e: e.activation(out=gl_sb[:, :], in_=pt_ps[:, 0:512], func=AF.Gelu),
                 r=[kptps], w=[kgl])
            b.op("vector", lambda e, c=c: e.tensor_mul(h_sb[:, c, :], a_sb[:, :], gl_sb[:, :]),
                 r=[ka, kgl], w=[kh])
        import os as _os2
        if _os2.environ.get("MKDEBUG") and g == 0:
            dbt_h = b.outp("dbg_btrow", (1, 5632), BF16)
            b.dma("sync", lambda e: e.dma_start(out=dbt_h[...], in_=btrow[:, :]), r=[kbt], tag="dbt")
            dh_h = b.outp("dbg_h", (128, 22, 512), BF16)
            b.dma("sync", lambda e: e.dma_start(out=dh_h[...], in_=h_sb[:, :, :]), r=[kh], tag="dh")
            dz_h = b.outp("dbg_z2n", (128, D), BF16)
            b.dma("sync", lambda e: e.dma_start(out=dz_h[...], in_=z2n[:, :]), r=[kz2n], tag="dz")
            dgc_h = b.outp("dbg_gcols", (128, 8))
            b.dma("sync", lambda e: e.dma_start(out=dgc_h[...], in_=gcols[:, :]), r=[kgc], tag="dgc")
        for tt in range(4):
            r0 = g * 512 + tt * 128
            b.dma("sync", lambda e, r0=r0: e.dma_start(out=yt[:, :], in_=y2_h[r0:r0 + 128, :]),
                  r=[("y2h", g)], w=[kyt], tag=f"flyr{g}{tt}")
            for m in range(2):
                for c in range(22):
                    b.op("tensor", lambda e, m=m, c=c, tt=tt: e.matmul(
                             o_ps[:, m * 512:(m + 1) * 512], h_sb[:, c, tt * 128:(tt + 1) * 128],
                             w2sb[:, c, m * 512:(m + 1) * 512], start=(c == 0), stop=False),
                         r=[kh, kw2], w=[kops])
                b.op("tensor", lambda e, m=m: e.matmul(
                         o_ps[:, m * 512:(m + 1) * 512], ones[0:1, 0:128], b2row[0:1, m * 512:(m + 1) * 512],
                         start=False, stop=True), r=[kon, kb2r], w=[kops])
            b.op("vector", lambda e: e.tensor_add(yt[:, :], o_ps[:, :], yt[:, :]),
                 r=[kops, kyt], w=[kyt])
            b.dma("sync", lambda e, r0=r0: e.dma_start(out=yo_h[r0:r0 + 128, :], in_=yt[:, :]),
                  r=[kyt], tag=f"fst{g}{tt}")
    return b


from concourse.bass_utils import run_bass_kernel_spmd

_PROGS = {}
NCORE = 8


def _get(name, builder):
    if name not in _PROGS:
        bb = builder(4096)
        nc = bb.finalize()
        consts = {k: v for k, v in bb.inputs.items() if v is not None}
        _PROGS[name] = (nc, consts)
    return _PROGS[name]


def _launch(name, builder, per_core, outs):
    nc, consts = _get(name, builder)
    in_maps = []
    for c in range(NCORE):
        m = dict(consts)
        m.update(per_core[c])
        in_maps.append(m)
    res = run_bass_kernel_spmd(nc, in_maps, core_ids=list(range(NCORE)))
    return [{k: np.asarray(res.results[c][k]) for k in outs} for c in range(NCORE)]


def kernel(**inp):
    f32 = lambda a: np.ascontiguousarray(np.asarray(a, dtype=np.float32))
    x_all = f32(inp["L0_attn_x"])
    xs = [x_all[c] for c in range(NCORE)]
    wg, wu, wd = f32(inp["L0_ffn_w_gate"]), f32(inp["L0_ffn_w_up"]), f32(inp["L0_ffn_w_down"])
    l0w = {n: f32(inp["L0_attn_W" + n[-1]]) for n in ("Wq", "Wk", "Wv", "Wo")}
    ncst = f32(inp["L0_attn_norm_const"])
    a1 = {"Wqkv": f32(inp["L1_attn_Wqkv"]), "bqkv": f32(inp["L1_attn_bqkv"]),
          "Wo": f32(inp["L1_attn_Wo"]), "bo": f32(inp["L1_attn_bo"])}
    f1 = {k: f32(inp["L1_ffn_" + k]) for k in
          ("Wqkv", "q_scale", "k_scale", "Wo", "ln_g", "ln_b", "W1", "b1", "W2", "b2")}

    def l0ffn_phase(xs):
        r = _launch("l0ffn", build_l0ffn,
                    [{"x": xs[c], "w_gate": wg, "w_up": wu, "w_down": wd} for c in range(NCORE)],
                    ["xo"])
        return [r[c]["xo"] for c in range(NCORE)]

    def l0attn_phase(xs):
        r = _launch("l0attn", build_l0attn,
                    [dict(x=xs[c], norm_const=ncst, **l0w) for c in range(NCORE)], ["xo"])
        return [r[c]["xo"] for c in range(NCORE)]

    def l1ffn_phase(xs):
        r = _launch("l1init", build_l1init, [{"x": xs[c]} for c in range(NCORE)], ["y"])
        ys = [r[c]["y"] for c in range(NCORE)]
        for i in range(6):
            wi = {"Wqkv": f1["Wqkv"][i], "qs": f1["q_scale"][i], "ks": f1["k_scale"][i],
                  "Wo_": f1["Wo"][i], "g_ln": f1["ln_g"][i], "b_ln": f1["ln_b"][i],
                  "W1": f1["W1"][i], "b1": f1["b1"][i], "W2": f1["W2"][i], "b2": f1["b2"][i]}
            r = _launch("l1iter", build_l1iter,
                        [dict(y=ys[c], **wi) for c in range(NCORE)], ["yo"])
            ys = [r[c]["yo"] for c in range(NCORE)]
        r = _launch("l1fin", build_l1fin,
                    [{"x": xs[c], "y": ys[c]} for c in range(NCORE)], ["xo"])
        return [r[c]["xo"] for c in range(NCORE)]

    def l1attn_phase(xs):
        r = _launch("l1attn", build_l1attn,
                    [dict(x=xs[c], **a1) for c in range(NCORE)], ["xo"])
        return [r[c]["xo"] for c in range(NCORE)]

    xs = l0ffn_phase(xs)
    xs = l0attn_phase(xs)
    xs = l0ffn_phase(xs)
    xs = l1ffn_phase(xs)
    xs = l1attn_phase(xs)
    xs = l1ffn_phase(xs)
    return np.stack(xs, 0).astype(np.float32)
